# version 01 (final)
# baseline (speedup 1.0000x reference)
import contextlib
import numpy as np

import concourse.bass as bass
import concourse.mybir as mybir
from concourse.bass_utils import run_bass_kernel_spmd

F32 = mybir.dt.float32
BF16 = mybir.dt.bfloat16
AF = mybir.ActivationFunctionType
ALU = mybir.AluOpType
AX = mybir.AxisListType

NCORES = 8
NB = 2
B = 16
T = 2048
D = 1024
C = 8
TC = 16
NT = 4
NEG = -1.0e9
EPS = 1e-5
ENGINES = ("sync", "tensor", "vector", "scalar", "gpsimd")


class Q:

    def __init__(self, nc, ctx):
        self.nc = nc
        self.ctx = ctx
        self.sems = {}
        self.cnt = {}
        self.qs = {e: [] for e in ENGINES}

    def sem(self, name):
        if name not in self.sems:
            self.sems[name] = self.ctx.enter_context(self.nc.semaphore(name))
            self.cnt[name] = 0
        return self.sems[name]

    def c(self, name):
        return self.cnt.get(name, 0)

    def q(self, engine, fn, waits=(), inc=None, n=1):
        wl = []
        for s, v in waits:
            self.sem(s)
            wl.append((s, self.cnt[s] if v is None else v))
        out = None
        if inc is not None:
            self.sem(inc)
            self.cnt[inc] += n
            out = self.cnt[inc]
        self.qs[engine].append((wl, fn, (inc, n) if inc else None))
        return out

    def play(self, block):
        for ename in ENGINES:
            items = self.qs[ename]
            if not items:
                continue

            def run(eng, items=items):
                for wl, fn, tok in items:
                    for s, v in wl:
                        if v > 0:
                            eng.wait_ge(self.sems[s], v)
                    inst = fn(eng)
                    if tok is not None:
                        inst.then_inc(self.sems[tok[0]], tok[1])
            getattr(block, ename)(run)


def build_program(layers=(0, 1, 2, 3), debug=False):
    nc = bass.Bass("TRN2", target_bir_lowering=False)
    ctx = contextlib.ExitStack()
    d = {}

    def din(name, shape, dtype=F32, kind="ExternalInput"):
        d[name] = nc.dram_tensor(name, shape, dtype, kind=kind).ap()
        return d[name]

    din("xin", [NB, D, T])
    if 0 in layers:
        din("xg", [B, D, T])
    din("y", [NB, D, T], kind="ExternalOutput")
    din("xres", [NB, D, T], kind="Internal")
    din("mrow", [4096], kind="Internal")
    din("eye_f", [128, 128]); din("iotab", [128, T]); din("pid", [128, 1])
    din("trilm", [128, 128], BF16); din("trium", [128, 128], BF16)
    din("globm", [128, 128], BF16); din("l1spm", [128, 128], BF16)
    din("sel8", [8, 1024]); din("selb0", [128, 8]); din("selb1", [128, 8])
    din("e16", [8, 128]); din("e16t", [128, 8])
    if debug:
        din("dbgA", [6, 128, 8192], BF16, kind="ExternalOutput")
        din("dbgS", [128, T], kind="ExternalOutput")
        din("dbgP", [128, T], BF16, kind="ExternalOutput")
        din("dbgT", [128, 16, 128], BF16, kind="ExternalOutput")
        din("dbgc", [3, 128, 1], kind="ExternalOutput")
    if 0 in layers:
        din("l0_wqt", [D, D]); din("l0_wkt", [D, D]); din("l0_wvt", [D, D])
        din("l0_wot", [D, D]); din("l0_wi1t", [D, 512]); din("l0_wi2", [128, 4])
        din("l0_wg", [D, 8]); din("l0_w1", [8, D, 2048]); din("l0_w3", [8, D, 2048])
        din("l0_w2", [8, 2048, D]); din("l0_bias", [128, 32]); din("l0_bi1", [128, 4])
    if 1 in layers:
        din("l1_wqkv", [D, 3072]); din("l1_wout", [D, D])
        din("l1_gwt", [D, 8]); din("l1_wu", [8, D, 2048]); din("l1_wd", [8, 2048, D])
        din("l1_ebias", [1, 8])
    if 2 in layers:
        din("l2_wqt", [D, D]); din("l2_wqkvt", [D, 3072]); din("l2_bqkv", [128, 24])
        din("l2_wopt", [D, 256]); din("l2_w2", [128, 2]); din("l2_b2", [1, 1])
        din("l2f_w1t", [D, 4096]); din("l2f_w2t", [D, 4096])
        din("l2f_h1t", [4096, 4096]); din("l2f_h2t", [4096, 4096])
        din("l2f_w3t", [4096, D])
        din("l2f_s1", [128, 32]); din("l2f_s2", [128, 32])
        din("l2f_sh1", [128, 32]); din("l2f_sh2", [128, 32]); din("l2f_s3", [128, 8])
        din("hscr", [4096, T], BF16, kind="Internal")
        din("gscr", [4096, T], BF16, kind="Internal")
        din("qscr", [D, T], BF16, kind="Internal")
    if 3 in layers:
        din("l3_wq", [D, D]); din("l3_wk", [D, D]); din("l3_wv", [D, D])
        din("l3_wo", [D, D]); din("l3_bias", [128, 8])
        din("l3_kern", [D, D]); din("l3_kbias", [128, 8])
        din("l3_a", [D, 16]); din("l3_b", [16, D])
        din("kqscr", [D, D], BF16, kind="Internal")

    with ctx:
        sb = lambda n, s, t: ctx.enter_context(nc.sbuf_tensor(n, s, t))
        psum = lambda n, s, t=F32: ctx.enter_context(nc.psum_tensor(n, s, t))

        AR = [sb(f"A{i}", [128, 4, 2048], BF16) for i in range(8)]
        S1 = sb("S1", [128, T], F32)
        S2 = sb("S2", [128, T], BF16)
        WST = sb("WST", [128, 2048], F32)
        WBF = sb("WBF", [128, 2048], BF16)
        OST = sb("OST", [128, 2, 512], F32)
        PE1 = sb("PE1", [128, T], BF16)
        PTB = sb("PTB", [128, 16, 128], BF16)
        DG = sb("DG", [128, 128], BF16)
        SCA = sb("SCA", [128, T], F32)
        c_eye = sb("c_eye", [128, 128], F32)
        c_eyeb = sb("c_eyeb", [128, 128], BF16)
        c_iotab = sb("c_iotab", [128, T], F32) if 2 in layers else None
        c_pid = sb("c_pid", [128, 1], F32)
        c_tril = sb("c_tril", [128, 128], BF16)
        c_triu = sb("c_triu", [128, 128], BF16)
        c_glob = sb("c_glob", [128, 128], BF16)
        c_l1sp = sb("c_l1sp", [128, 128], BF16)
        c_sel8 = sb("c_sel8", [8, 1024], F32)
        c_selb = [sb("c_selb0", [128, 8], F32), sb("c_selb1", [128, 8], F32)]
        c_e16 = sb("c_e16", [8, 128], F32)
        c_e16t = sb("c_e16t", [128, 8], F32)
        onesf = sb("onesf", [128, 1], F32)
        onesrow = sb("onesrow", [1, 128], F32)
        ones8b = sb("ones8b", [8, 1], BF16)
        U1 = sb("U1", [8, 2048], F32)
        ROW = S1[0:1, 0:2048].rearrange("p (a t) -> p a t", t=512)
        negm = sb("negm", [128, 1], F32)
        sums = sb("sums", [128, 1], F32)
        rcol = sb("rcol", [128, 1], F32)
        tcol = sb("tcol", [128, 1], F32)
        qcol = sb("qcol", [128, 1], F32)
        btmp = sb("btmp", [128, 8], F32)
        lo8 = [sb("lo8_0", [8, 1], F32), sb("lo8_1", [8, 1], F32)]
        hi8 = [sb("hi8_0", [8, 1], F32), sb("hi8_1", [8, 1], F32)]
        mid8 = sb("mid8", [8, 1], F32)
        cn8 = sb("cn8", [8, 1], F32)
        c018 = sb("c018", [8, 1], F32)
        t8a = sb("t8a", [8, 1], F32)
        t8b = sb("t8b", [8, 1], F32)
        loA = [sb("loA_0", [1, 1], F32), sb("loA_1", [1, 1], F32)]
        hiA = [sb("hiA_0", [1, 1], F32), sb("hiA_1", [1, 1], F32)]
        midA = sb("midA", [1, 1], F32)
        c01A = sb("c01A", [1, 1], F32)
        tAa = sb("tAa", [1, 1], F32)
        tAb = sb("tAb", [1, 1], F32)
        ccol = sb("ccol", [128, 1], F32)
        LU = sb("LU", [16, T], BF16)
        LB = sb("LB", [16, 1024], BF16)
        wgb = sb("wgb", [128, 8, 8], BF16)
        WGF = sb("WGF", [128, 8, 8], F32)
        SCL = sb("SCL", [128, 136], F32) if 2 in layers else None
        DXG = sb("DXG", [128, 16, 4], F32) if 2 in layers else None
        RQ = sb("RQ", [128, 2, 512], BF16) if 2 in layers else None
        BIA = sb("BIA", [128, 32], F32)
        BI1 = sb("BI1", [128, 4], F32)
        epsc = sb("epsc", [1, 1], F32)
        nwg1 = sb("nwg1", [1, 8], BF16)
        nwg1f = sb("nwg1f", [1, 8], F32)
        negc8 = sb("negc8", [8, 1], F32)
        g8f = U1[0:8, 0:512]
        g8s = U1[0:8, 512:1024]
        g8e = PE1[0:8, 0:512]
        rrow = U1[0:1, 1024:1536]
        mrowbf = sb("mrowbf", [1, 512], BF16)
        improw = U1[0:1, 0:2048]
        impc = sb("impc", [128, 16], F32)
        wi2b = sb("wi2b", [128, 4, 1], BF16)
        tsil = sb("tsil", [128, 512], BF16)

        LU = PE1[0:16, :]
        LB = WBF[0:16, 1024:2048]
        psA = psum("psA", [128, T], F32)
        psB = psum("psB", [128, 1536], F32)
        psT = psum("psT", [128, 1024], BF16)

        block = ctx.enter_context(nc.Block())
        Qz = Q(nc, ctx)
        qq = Qz.q

        def w(*sems):
            return [(s, None) for s in sems]

        A = lambda i: psA[:, i * 512:(i + 1) * 512]
        Bk = lambda i: psB[:, i * 512:(i + 1) * 512]

        def arv(lo, c):
            return AR[lo + c // 4][:, c % 4, :]

        def vtv(lo, tc):
            return AR[lo + tc // 8][:].rearrange("p a t -> p (a t)") \
                [:, (tc % 8) * 1024:(tc % 8) * 1024 + 1024]

        def loadcast(src_ap, a, wdt, woff=0):
            stv = WST[:, 0:a * wdt].rearrange("p (a w) -> p a w", w=wdt)
            bfv = WBF[:, woff:woff + a * wdt].rearrange("p (a w) -> p a w", w=wdt)
            qq("sync", lambda e, s=src_ap, stv=stv: e.dma_start(
                out=stv, in_=s.rearrange("(a p) w -> p a w", p=128)),
                waits=w("ve", "mm"), inc="ldw", n=16)
            qq("vector", lambda e, stv=stv, bfv=bfv: e.tensor_copy(bfv, stv),
                waits=w("ldw"), inc="ve")
            return bfv

        def colload(src_ap, dst, scale=None):
            qq("sync", lambda e, s=src_ap, dst=dst: e.dma_start(out=dst, in_=s),
               waits=w("ve", "ac"), inc="ld", n=16)
            if scale is not None:
                qq("vector", lambda e, dst=dst, sc=scale: e.tensor_scalar(
                    dst, dst, sc, None, ALU.mult, ALU.bypass),
                    waits=w("ld"), inc="ve")

        def ph_consts():
            loads = [("eye_f", c_eye), ("pid", c_pid),
                     ("trilm", c_tril), ("trium", c_triu), ("globm", c_glob),
                     ("l1spm", c_l1sp), ("sel8", c_sel8), ("selb0", c_selb[0]),] + ([("iotab", c_iotab)] if 2 in layers else []) + [
                     ("selb1", c_selb[1]), ("e16", c_e16), ("e16t", c_e16t)]
            for src, dst in loads:
                qq("sync", lambda e, s=src, t_=dst: e.dma_start(out=t_[:], in_=d[s]),
                   inc="ldc", n=16)
            qq("sync", lambda e: e.dma_start(out=d["xres"], in_=d["xin"]), inc="ldc", n=16)
            qq("vector", lambda e: e.memset(onesf[:], 1.0), inc="ve")
            qq("vector", lambda e: e.memset(onesrow[:], 1.0), inc="ve")
            qq("vector", lambda e: e.memset(ones8b[:], 1.0), inc="ve")
            qq("vector", lambda e: e.memset(epsc[:], EPS), inc="ve")
            qq("vector", lambda e: e.tensor_copy(c_eyeb[:], c_eye[:]),
               waits=w("ldc"), inc="ve")

        def ln_stats_subtile(xsrc, sl, extra_per_chunk=None):
            for cch in range(C):
                slot = S1[:, (cch % 4) * 512:(cch % 4) * 512 + 512]
                qq("sync", lambda e, cch=cch, slot=slot: e.dma_start(
                    out=slot, in_=xsrc(cch)[:, sl]),
                    waits=w("mm", "ac", "ve", "gd"), inc="lds", n=16)
                qq("scalar", lambda e, cch=cch, slot=slot: e.activation(
                    OST[:, cch % 2, :], slot, AF.Square),
                    waits=w("lds", "gd"), inc="ac")

                def mmfn(e, cch=cch, slot=slot):
                    e.matmul(psA[0:1, 0:512], onesf[:], slot,
                             start=(cch == 0), stop=(cch == C - 1))
                    return e.matmul(psA[0:1, 512:1024], onesf[:], OST[:, cch % 2, :],
                                    start=(cch == 0), stop=(cch == C - 1))
                qq("tensor", mmfn, waits=w("lds", "ac"), inc="mm")
                if extra_per_chunk is not None:
                    extra_per_chunk(cch, slot)

            def rowsfn(e):
                e.activation(ROW[:, 0, :], psA[0:1, 0:512], AF.Copy, scale=1.0 / D)
                return e.activation(ROW[:, 1, :], psA[0:1, 512:1024], AF.Copy, scale=1.0 / D)
            qq("scalar", rowsfn, waits=w("mm"), inc="ac")
            qq("vector", lambda e: e.tensor_tensor(ROW[:, 2, :], ROW[:, 0, :], ROW[:, 0, :], ALU.mult),
               waits=w("ac"), inc="ve")
            qq("vector", lambda e: e.tensor_tensor(ROW[:, 3, :], ROW[:, 1, :], ROW[:, 2, :], ALU.subtract),
               waits=w("ve"), inc="ve")
            qq("scalar", lambda e: e.activation(ROW[:, 2, :], ROW[:, 3, :], AF.Sqrt, bias=epsc[0:1, :]),
               waits=w("ve"), inc="ac")
            qq("vector", lambda e: e.reciprocal(ROW[:, 3, :], ROW[:, 2, :]),
               waits=w("ac"), inc="ve")
            qq("vector", lambda e: e.tensor_tensor(ROW[:, 2, :], ROW[:, 0, :], ROW[:, 3, :], ALU.mult),
               waits=w("ve"), inc="ve")

        def ph_ln(xsrc, dst_lo, gate=False):
            if gate:
                qq("sync", lambda e: e.dma_start(
                    out=WGF[:], in_=d["l1_gwt"].rearrange("(a p) e -> p a e", p=128)),
                    waits=w("ve", "ac", "mm"), inc="ldm", n=16)

                def wg1fn(e):
                    for cch in range(C):
                        inst = e.matmul(psB[0:1, 0:8], onesf[:], WGF[:, cch, :],
                                        start=(cch == 0), stop=(cch == C - 1))
                    return inst
                qq("tensor", wg1fn, waits=w("ldm"), inc="mm")
                qq("vector", lambda e: e.tensor_scalar(nwg1f[:], psB[0:1, 0:8], -1.0, None,
                                                       ALU.mult, ALU.bypass),
                   waits=w("mm"), inc="ve")
            for nt in range(NT):
                sl = slice(nt * 512, (nt + 1) * 512)
                ln_stats_subtile(xsrc, sl)

                def bcfn(e):
                    e.matmul(Bk(0), onesrow[:], ROW[:, 3, :], start=True, stop=True)
                    return e.matmul(Bk(1), onesrow[:], ROW[:, 2, :], start=True, stop=True)
                qq("tensor", bcfn, waits=w("ve"), inc="mm")
                if gate:
                    qq("vector", lambda e: e.tensor_copy(WST[0:1, 1024:1536], ROW[:, 0, :]),
                       waits=w("ve", "mm"), inc="ve")
                    qq("vector", lambda e: e.tensor_copy(WST[0:1, 1536:2048], ROW[:, 3, :]),
                       waits=w("ve"), inc="ve")
                for cch in range(C):
                    slot = S1[:, (cch % 4) * 512:(cch % 4) * 512 + 512]
                    qq("sync", lambda e, cch=cch, slot=slot, sl=sl: e.dma_start(
                        out=slot, in_=xsrc(cch)[:, sl]),
                        waits=w("mm", "ve", "ac", "gd"), inc="lds", n=16)
                    if gate:
                        qq("tensor", lambda e, cch=cch, slot=slot: e.matmul(
                            psB[0:8, 1024:1536], WGF[:, cch, :], slot,
                            start=(cch == 0), stop=False), waits=w("lds"), inc="mm")
                    qq("vector", lambda e, cch=cch, slot=slot: e.scalar_tensor_tensor(
                        OST[:, cch % 2, :], slot, 1.0, Bk(0), ALU.bypass, ALU.mult),
                        waits=w("lds", "mm", "gd"), inc="ve")
                    qq("vector", lambda e, cch=cch, sl=sl: e.tensor_tensor(
                        arv(dst_lo, cch)[:, sl], OST[:, cch % 2, :], Bk(1), ALU.subtract),
                        waits=w("ve"), inc="ve")
                if gate:
                    def gclose(e):
                        e.matmul(psB[0:8, 1024:1536], nwg1f[:], WST[0:1, 1024:1536],
                                 start=False, stop=True)
                        return e.matmul(psA[0:8, 0:512], onesrow[0:1, 0:8], WST[0:1, 1536:2048],
                                        start=True, stop=True)
                    m5 = qq("tensor", gclose, waits=w("ve"), inc="mm")
                    a5 = qq("scalar", lambda e: e.copy(WST[0:8, 0:512], psB[0:8, 1024:1536]),
                            waits=[("mm", m5)], inc="ac")
                    v5 = qq("vector", lambda e: e.tensor_tensor(
                        WST[0:8, 512:1024], WST[0:8, 0:512], psA[0:8, 0:512], ALU.mult),
                        waits=[("ac", a5)], inc="ve")
                    qq("scalar", lambda e, sl=sl: e.activation(
                        U1[0:8, sl], WST[0:8, 512:1024], AF.Sigmoid),
                        waits=[("ve", v5)], inc="ac")

        def ph_gating():
            qq("sync", lambda e: e.dma_start(
                out=WST[:, 0:64].rearrange("p (a e) -> p a e", e=8),
                in_=d["l0_wg"].rearrange("(a p) e -> p a e", p=128)),
                waits=w("ve", "ac", "mm"), inc="ldm", n=16)
            qq("vector", lambda e: e.tensor_copy(
                wgb[:], WST[:, 0:64].rearrange("p (a e) -> p a e", e=8)),
                waits=w("ldm"), inc="ve")
            qq("scalar", lambda e: e.activation(OST[:, 0, 0:64], WST[:, 0:64], AF.Square),
               waits=w("ldm", "gd"), inc="ac")

            def wg1fn(e):
                for cch in range(C):
                    e.matmul(psB[0:1, 0:8], onesf[:],
                             WST[:, 0:64].rearrange("p (a e) -> p a e", e=8)[:, cch, :],
                             start=(cch == 0), stop=(cch == C - 1))
                for cch in range(C):
                    inst = e.matmul(psB[0:1, 8:16], onesf[:],
                                    OST[:, 0, 0:64].rearrange("p (a e) -> p a e", e=8)[:, cch, :],
                                    start=(cch == 0), stop=(cch == C - 1))
                return inst
            qq("tensor", wg1fn, waits=w("ve", "ac"), inc="mm")
            qq("vector", lambda e: e.tensor_scalar(nwg1f[:], psB[0:1, 0:8], -1.0, None, ALU.mult, ALU.bypass),
               waits=w("mm"), inc="ve")
            qq("vector", lambda e: e.tensor_copy(nwg1[:], nwg1f[:]), waits=w("ve"), inc="ve")
            qq("scalar", lambda e: e.activation(ROW[:, 0, 0:8], psB[0:1, 8:16], AF.Sqrt),
               waits=w("mm"), inc="ac")
            qq("vector", lambda e: e.reduce_max(tAa[:], ROW[:, 0, 0:8], AX.X),
               waits=w("ac"), inc="ve")
            qq("vector", lambda e: e.tensor_scalar(tAb[:], tAa[:], -32.0, None, ALU.mult, ALU.bypass),
               waits=w("ve"), inc="ve")
            qq("tensor", lambda e: e.matmul(psB[0:8, 1024:1025], onesrow[0:1, 0:8], tAb[:],
                                            start=True, stop=True), waits=w("ve"), inc="mm")
            qq("scalar", lambda e: e.copy(negc8[:], psB[0:8, 1024:1025]), waits=w("mm"), inc="ac")

            for g in range(B):
                for nt in range(NT):
                    sl = slice(nt * 512, (nt + 1) * 512)

                    def cast_chunk(cch, slot, g=g, nt=nt):
                        qq("vector", lambda e, cch=cch, slot=slot: e.tensor_copy(
                            arv(0, cch)[:, nt * 512:(nt + 1) * 512], slot),
                            waits=w("lds"), inc="ve")
                    ln_stats_subtile(lambda cch, g=g: d["xg"][g, cch * 128:(cch + 1) * 128, :],
                                     sl, extra_per_chunk=cast_chunk)
                    qq("vector", lambda e: e.tensor_copy(mrowbf[:], ROW[:, 0, :]),
                       waits=w("ve"), inc="ve")

                    def g8fn(e, nt=nt):
                        for cch in range(C):
                            e.matmul(psA[0:8, 1024:1536], wgb[:, cch, :],
                                     arv(0, cch)[:, nt * 512:(nt + 1) * 512],
                                     start=(cch == 0), stop=False)
                        return e.matmul(psA[0:8, 1024:1536], nwg1[:], mrowbf[:],
                                        start=False, stop=True)
                    qq("tensor", g8fn, waits=w("ve"), inc="mm")
                    qq("tensor", lambda e: e.matmul(psB[0:8, 0:512], onesrow[0:1, 0:8],
                                                    ROW[:, 3, :], start=True, stop=True),
                       waits=w("ve"), inc="mm")
                    qq("scalar", lambda e: e.copy(g8f, psA[0:8, 1024:1536]),
                       waits=w("mm", "gd"), inc="ac")
                    qq("vector", lambda e: e.tensor_tensor(g8s, g8f, psB[0:8, 0:512], ALU.mult),
                       waits=w("ac"), inc="ve")
                    qq("scalar", lambda e: e.activation(g8e, g8s, AF.Exp, bias=negc8[0:8, :]),
                       waits=w("ve"), inc="ac")
                    qq("tensor", lambda e: e.matmul(psB[0:1, 512:1024], ones8b[:], g8e[:],
                                                    start=True, stop=True), waits=w("ac"), inc="mm")
                    qq("vector", lambda e: e.reciprocal(rrow, psB[0:1, 512:1024]),
                       waits=w("mm", "gd"), inc="ve")
                    qq("tensor", lambda e: e.matmul(psB[0:8, 1024:1536], onesrow[0:1, 0:8], rrow,
                                                    start=True, stop=True), waits=w("ve"), inc="mm")
                    qq("vector", lambda e: e.tensor_tensor(g8s, g8e, psB[0:8, 1024:1536], ALU.mult),
                       waits=w("mm"), inc="ve")
                    qq("gpsimd", lambda e, g=g, nt=nt: e.dma_start(
                        out=SCA[g * 8:(g + 1) * 8, nt * 512:(nt + 1) * 512],
                        in_=g8s), waits=w("ve", "ldc"), inc="gd", n=16)

        def ph_t8():
            qq("vector", lambda e: e.memset(lo8[0][:], 0.0), waits=w("gd"), inc="ve")
            qq("vector", lambda e: e.memset(hi8[0][:], 1.0), inc="ve")
            for it in range(30):
                pa, pb_ = it % 2, (it + 1) % 2
                qq("vector", lambda e, pa=pa: e.tensor_tensor(t8a[:], lo8[pa][:], hi8[pa][:], ALU.add),
                   waits=w("ve", "ac"), inc="ve")
                qq("vector", lambda e: e.tensor_scalar(mid8[:], t8a[:], 0.5, None, ALU.mult, ALU.bypass),
                   waits=w("ve"), inc="ve")
                qq("tensor", lambda e: e.matmul(psB[0:128, 1024:1025], c_e16[:], mid8[:],
                                                start=True, stop=True), waits=w("ve"), inc="mm")
                qq("scalar", lambda e: e.copy(tcol[:], psB[0:128, 1024:1025]), waits=w("mm"), inc="ac")
                qq("vector", lambda e: e.tensor_scalar(PE1[:], SCA[:], tcol[:], None, ALU.is_ge,
                                                       ALU.add, accum_out=ccol[:]),
                   waits=w("ac"), inc="ve")
                qq("tensor", lambda e: e.matmul(psB[0:8, 1025:1026], c_e16t[:], ccol[:],
                                                start=True, stop=True), waits=w("ve"), inc="mm")
                qq("scalar", lambda e: e.copy(cn8[:], psB[0:8, 1025:1026]), waits=w("mm"), inc="ac")
                qq("vector", lambda e: e.tensor_scalar(c018[:], cn8[:], 4096.0, None, ALU.is_ge, ALU.bypass),
                   waits=w("ac"), inc="ve")
                qq("vector", lambda e, pa=pa: e.tensor_tensor(t8a[:], mid8[:], lo8[pa][:], ALU.subtract),
                   waits=w("ve"), inc="ve")
                qq("vector", lambda e: e.tensor_tensor(t8b[:], t8a[:], c018[:], ALU.mult),
                   waits=w("ve"), inc="ve")
                qq("vector", lambda e, pa=pa, pb_=pb_: e.tensor_tensor(lo8[pb_][:], lo8[pa][:], t8b[:], ALU.add),
                   waits=w("ve"), inc="ve")
                qq("vector", lambda e, pa=pa: e.tensor_tensor(t8a[:], hi8[pa][:], mid8[:], ALU.subtract),
                   waits=w("ve"), inc="ve")
                qq("vector", lambda e: e.tensor_scalar(t8b[:], c018[:], -1.0, 1.0, ALU.mult, ALU.add),
                   waits=w("ve"), inc="ve")
                qq("vector", lambda e: e.tensor_tensor(mid8[:], t8a[:], t8b[:], ALU.mult),
                   waits=w("ve"), inc="ve")
                qq("vector", lambda e, pa=pa, pb_=pb_: e.tensor_tensor(hi8[pb_][:], hi8[pa][:], mid8[:], ALU.subtract),
                   waits=w("ve"), inc="ve")
            qq("tensor", lambda e: e.matmul(psB[0:128, 1024:1025], c_e16[:], lo8[0][:],
                                            start=True, stop=True), waits=w("ve"), inc="mm")
            qq("scalar", lambda e: e.copy(tcol[:], psB[0:128, 1024:1025]), waits=w("mm"), inc="ac")
            qq("vector", lambda e: e.tensor_scalar(PE1[:], SCA[:], tcol[:], None, ALU.is_ge, ALU.bypass),
               waits=w("ac"), inc="ve")
            qq("vector", lambda e: e.tensor_tensor(S1[:], PE1[:], SCA[:], ALU.mult),
               waits=w("ve"), inc="ve")
            qq("vector", lambda e: e.tensor_copy(SCA[:], S1[:]), waits=w("ve"), inc="ve")

        def ph_l0_keymask(xn_lo):
            h4 = AR[2]
            qq("sync", lambda e: e.dma_start(out=BI1[:], in_=d["l0_bi1"]),
               waits=w("ve", "ac", "mm"), inc="ldm", n=16)
            for oc in range(4):
                wv = loadcast(d["l0_wi1t"][:, oc * 128:(oc + 1) * 128], C, 128)
                for nt in range(NT):
                    slot = (nt % 2) * 512

                    def mmfn(e, wv=wv, nt=nt, slot=slot):
                        for cch in range(C):
                            inst = e.matmul(psA[:, slot:slot + 512], wv[:, cch, :],
                                            arv(xn_lo, cch)[:, nt * 512:(nt + 1) * 512],
                                            start=(cch == 0), stop=(cch == C - 1))
                        return inst
                    qq("tensor", mmfn, waits=w("ve", "ac"), inc="mm")
                    qq("scalar", lambda e, oc=oc, nt=nt, slot=slot: e.activation(
                        h4[:, oc, nt * 512:(nt + 1) * 512], psA[:, slot:slot + 512], AF.Relu,
                        bias=BI1[:, oc:oc + 1]),
                        waits=w("mm", "ldm"), inc="ac")
            qq("sync", lambda e: e.dma_start(out=WST[:, 0:4], in_=d["l0_wi2"]),
                waits=w("ve", "mm", "ac"), inc="ldm", n=16)
            qq("vector", lambda e: e.tensor_copy(wi2b[:], WST[:, 0:4].rearrange("p (a w) -> p a w", w=1)),
               waits=w("ldm"), inc="ve")
            for tc in range(TC):
                def mmfn(e, tc=tc):
                    for a in range(4):
                        inst = e.matmul(psB[:, 1024 + (tc % 2):1025 + (tc % 2)],
                                        h4[:, a, tc * 128:(tc + 1) * 128], wi2b[:, a, :],
                                        start=(a == 0), stop=(a == 3))
                    return inst
                qq("tensor", mmfn, waits=w("ve", "ac"), inc="mm")
                qq("scalar", lambda e, tc=tc: e.copy(impc[:, tc:tc + 1],
                                                     psB[:, 1024 + (tc % 2):1025 + (tc % 2)]),
                   waits=w("mm"), inc="ac")
            qq("tensor", lambda e: e.transpose(psB[0:16, 0:128], impc[:], c_eye[:]),
               waits=w("ac"), inc="mm")
            qq("scalar", lambda e: e.copy(OST[0:16, 0, 0:128], psB[0:16, 0:128]),
               waits=w("mm", "gd"), inc="ac")
            qq("gpsimd", lambda e: e.dma_start(
                out=d["mrow"][0:2048].rearrange("(a t) -> a t", t=128), in_=OST[0:16, 0, 0:128]),
                waits=w("ac"), inc="gd", n=16)
            qq("gpsimd", lambda e: e.dma_start(
                out=improw, in_=d["mrow"][0:2048].rearrange("(o t) -> o t", o=1)),
                waits=w("gd"), inc="gd", n=16)

            def bcfn(e):
                for nt in range(NT):
                    inst = e.matmul(A(nt), onesrow[:], improw[0:1, nt * 512:(nt + 1) * 512],
                                    start=True, stop=True)
                return inst
            qq("tensor", bcfn, waits=w("gd", "ac", "ve"), inc="mm")
            qq("scalar", lambda e: e.copy(S1[:], psA[:]), waits=w("mm"), inc="ac")
            qq("vector", lambda e: e.tensor_reduce(tAa[:], improw, AX.X, ALU.min),
               waits=w("gd"), inc="ve")
            qq("vector", lambda e: e.reduce_max(tAb[:], improw, AX.X), waits=w("ve"), inc="ve")
            qq("vector", lambda e: e.tensor_copy(loA[0][:], tAa[:]), waits=w("ve"), inc="ve")
            qq("vector", lambda e: e.tensor_scalar(hiA[0][:], tAb[:], 1.0, None, ALU.add, ALU.bypass),
               waits=w("ve"), inc="ve")
            for it in range(35):
                pa, pb_ = it % 2, (it + 1) % 2
                qq("vector", lambda e, pa=pa: e.tensor_tensor(tAa[:], loA[pa][:], hiA[pa][:], ALU.add),
                   waits=w("ve", "ac"), inc="ve")
                qq("vector", lambda e: e.tensor_scalar(midA[:], tAa[:], 0.5, None, ALU.mult, ALU.bypass),
                   waits=w("ve"), inc="ve")
                qq("tensor", lambda e: e.matmul(psB[0:128, 1024:1025], onesrow[:], midA[:],
                                                start=True, stop=True), waits=w("ve"), inc="mm")
                qq("scalar", lambda e: e.copy(qcol[:], psB[0:128, 1024:1025]), waits=w("mm"), inc="ac")
                qq("vector", lambda e: e.tensor_scalar(PE1[:], S1[:], qcol[:], None, ALU.is_ge,
                                                       ALU.add, accum_out=ccol[:]),
                   waits=w("ac"), inc="ve")
                qq("vector", lambda e: e.tensor_scalar(c01A[:], ccol[0:1, :], 614.0, None,
                                                       ALU.is_ge, ALU.bypass),
                   waits=w("ve"), inc="ve")
                qq("vector", lambda e, pa=pa: e.tensor_tensor(tAa[:], midA[:], loA[pa][:], ALU.subtract),
                   waits=w("ve"), inc="ve")
                qq("vector", lambda e: e.tensor_tensor(tAb[:], tAa[:], c01A[:], ALU.mult),
                   waits=w("ve"), inc="ve")
                qq("vector", lambda e, pa=pa, pb_=pb_: e.tensor_tensor(loA[pb_][:], loA[pa][:], tAb[:], ALU.add),
                   waits=w("ve"), inc="ve")
                qq("vector", lambda e, pa=pa: e.tensor_tensor(tAa[:], hiA[pa][:], midA[:], ALU.subtract),
                   waits=w("ve"), inc="ve")
                qq("vector", lambda e: e.tensor_scalar(tAb[:], c01A[:], -1.0, 1.0, ALU.mult, ALU.add),
                   waits=w("ve"), inc="ve")
                qq("vector", lambda e: e.tensor_tensor(midA[:], tAa[:], tAb[:], ALU.mult),
                   waits=w("ve"), inc="ve")
                qq("vector", lambda e, pa=pa, pb_=pb_: e.tensor_tensor(hiA[pb_][:], hiA[pa][:], midA[:], ALU.subtract),
                   waits=w("ve"), inc="ve")
            qq("tensor", lambda e: e.matmul(psB[0:128, 1024:1025], onesrow[:], loA[35 % 2][:],
                                            start=True, stop=True), waits=w("ve"), inc="mm")
            qq("scalar", lambda e: e.copy(qcol[:], psB[0:128, 1024:1025]), waits=w("mm"), inc="ac")
            qq("vector", lambda e: e.tensor_scalar(S2[:], S1[:], qcol[:], NEG, ALU.is_lt, ALU.mult),
               waits=w("ac"), inc="ve")

        def ph_l0_ffn(bi, xn_lo):
            def selfn(e, bi=bi):
                for nt in range(NT):
                    inst = e.matmul(psA[0:8, nt * 512:(nt + 1) * 512], c_selb[bi][:],
                                    SCA[:, nt * 512:(nt + 1) * 512], start=True, stop=True)
                return inst
            qq("tensor", selfn, waits=w("ve", "ac"), inc="mm")
            qq("scalar", lambda e: e.copy(S1[0:8, :], psA[0:8, :]), waits=w("mm"), inc="ac")

            def hvv(oc):
                return AR[4 + oc // 4][:, oc % 4, :]

            for ex in range(8):
                for nt in range(NT):
                    sl = slice(nt * 512, (nt + 1) * 512)
                    qq("tensor", lambda e, ex=ex, sl=sl: e.matmul(
                        psB[:, 1024:1536], c_sel8[:, ex * 128:(ex + 1) * 128], S1[0:8, sl],
                        start=True, stop=True), waits=w("ve", "ac"), inc="mm")

                    def xefn(e, sl=sl):
                        for cch in range(C):
                            inst = e.tensor_tensor(arv(2, cch)[:, sl],
                                                   arv(xn_lo, cch)[:, sl],
                                                   psB[:, 1024:1536], ALU.mult)
                        return inst
                    qq("vector", xefn, waits=w("mm"), inc="ve")
                for oc in range(16):
                    w1v = loadcast(d["l0_w1"][ex, :, oc * 128:(oc + 1) * 128], C, 128, 0)
                    w3v = loadcast(d["l0_w3"][ex, :, oc * 128:(oc + 1) * 128], C, 128, 1024)
                    for nt in range(NT):
                        sl = slice(nt * 512, (nt + 1) * 512)

                        def z1fn(e, w1v=w1v, sl=sl):
                            for cch in range(C):
                                inst = e.matmul(A(0), w1v[:, cch, :], arv(2, cch)[:, sl],
                                                start=(cch == 0), stop=(cch == C - 1))
                            return inst
                        m1 = qq("tensor", z1fn, waits=w("ve", "ac"), inc="mm")

                        def z3fn(e, w3v=w3v, sl=sl):
                            for cch in range(C):
                                inst = e.matmul(A(1), w3v[:, cch, :], arv(2, cch)[:, sl],
                                                start=(cch == 0), stop=(cch == C - 1))
                            return inst
                        m2 = qq("tensor", z3fn, inc="mm")
                        qq("scalar", lambda e: e.activation(tsil[:], A(0), AF.Silu),
                           waits=[("mm", m1)], inc="ac")
                        qq("vector", lambda e, oc=oc, sl=sl: e.tensor_tensor(
                            hvv(oc)[:, sl], tsil[:], A(1), ALU.mult),
                            waits=[("mm", m2)] + w("ac"), inc="ve")
                for oc in range(8):
                    w2v = loadcast(d["l0_w2"][ex, :, oc * 128:(oc + 1) * 128], 16, 128, 0)
                    for nt in range(NT):
                        sl = slice(nt * 512, (nt + 1) * 512)
                        par = nt % 2

                        def outfn(e, w2v=w2v, sl=sl):
                            for hc in range(16):
                                inst = e.matmul(Bk(0), w2v[:, hc, :], hvv(hc)[:, sl],
                                                start=(hc == 0), stop=(hc == 15))
                            return inst
                        m3 = qq("tensor", outfn, waits=w("ve", "ac"), inc="mm")
                        qq("scalar", lambda e, par=par: e.copy(OST[:, par, :], Bk(0)),
                           waits=[("mm", m3)] + w("gd"), inc="ac")
                        qq("gpsimd", lambda e, bi=bi, oc=oc, sl=sl, par=par: e.dma_start(
                            out=d["xres"][bi, oc * 128:(oc + 1) * 128, sl],
                            in_=OST[:, par, :], accum_op=ALU.add),
                            waits=w("ac"), inc="gd", n=16)

        def qkv_proj(xn_lo, wqn, wkn, wvn, qscale, qb=None, kb=None, vb=None,
                     kscale=1.0, qoff=0, koff=0, voff=0, q_lo=2, k_lo=4, vt_lo=6):
            for wname, dst_lo, scale, boff, woff_ in ((wqn, q_lo, qscale, qb, qoff), (wkn, k_lo, kscale, kb, koff)):
                for oc in range(C):
                    wv = loadcast(d[wname][:, woff_ + oc * 128:woff_ + (oc + 1) * 128], C, 128)
                    if boff is not None:
                        qq("vector", lambda e, scale=scale, boff=boff, oc=oc: e.tensor_scalar(
                            btmp[:, 1:2], BIA[:, boff + oc:boff + oc + 1], scale, None, ALU.mult, ALU.bypass),
                            waits=w("ldm", "ac"), inc="ve")
                    for nt in range(NT):
                        sl = slice(nt * 512, (nt + 1) * 512)
                        slot = (nt % 2) * 512

                        def pfn(e, wv=wv, sl=sl, slot=slot):
                            for cch in range(C):
                                inst = e.matmul(psA[:, slot:slot + 512], wv[:, cch, :],
                                                arv(xn_lo, cch)[:, sl],
                                                start=(cch == 0), stop=(cch == C - 1))
                            return inst
                        m1 = qq("tensor", pfn, waits=w("ve", "ac"), inc="mm")
                        if boff is not None:
                            qq("scalar", lambda e, dst_lo=dst_lo, oc=oc, sl=sl, slot=slot, scale=scale:
                               e.activation(arv(dst_lo, oc)[:, sl], psA[:, slot:slot + 512],
                                            AF.Identity, bias=btmp[:, 1:2], scale=scale),
                               waits=[("mm", m1)], inc="ac")
                        else:
                            qq("scalar", lambda e, dst_lo=dst_lo, oc=oc, sl=sl, slot=slot, scale=scale:
                               e.activation(arv(dst_lo, oc)[:, sl], psA[:, slot:slot + 512],
                                            AF.Copy, scale=scale),
                               waits=[("mm", m1)], inc="ac")
            for oc in range(C):
                wv = loadcast(d[wvn][:, voff + oc * 128:voff + (oc + 1) * 128], C, 128)
                for nt in range(NT):
                    sl = slice(nt * 512, (nt + 1) * 512)
                    slot = (nt % 2) * 512

                    def pfn(e, wv=wv, sl=sl, slot=slot):
                        for cch in range(C):
                            inst = e.matmul(psA[:, slot:slot + 512], wv[:, cch, :],
                                            arv(xn_lo, cch)[:, sl],
                                            start=(cch == 0), stop=(cch == C - 1))
                        return inst
                    m1 = qq("tensor", pfn, waits=w("ve", "ac", "ldm"), inc="mm")
                    if vb is not None:
                        qq("scalar", lambda e, sl=sl, slot=slot, oc=oc, vb=vb: e.activation(
                            PE1[:, sl], psA[:, slot:slot + 512], AF.Identity,
                            bias=BIA[:, vb + oc:vb + oc + 1]),
                            waits=[("mm", m1)], inc="ac")
                    else:
                        qq("scalar", lambda e, sl=sl, slot=slot: e.copy(
                            PE1[:, sl], psA[:, slot:slot + 512]),
                            waits=[("mm", m1)], inc="ac")
                for tc in range(TC):
                    m2 = qq("tensor", lambda e, tc=tc: e.transpose(
                        psT[:, (tc % 8) * 128:(tc % 8) * 128 + 128],
                        PE1[:, tc * 128:(tc + 1) * 128], c_eyeb[:]),
                        waits=w("ac", "ve"), inc="mm")
                    qq("vector", lambda e, oc=oc, tc=tc, vt_lo=vt_lo: e.tensor_copy(
                        vtv(vt_lo, tc)[:, oc * 128:(oc + 1) * 128],
                        psT[:, (tc % 8) * 128:(tc % 8) * 128 + 128]),
                        waits=[("mm", m2)], inc="ve")

        def attn_inner(out_lo, kcs_of_qc, mask_emit, nheads=16):
            for qc in range(TC):
                kcs = kcs_of_qc(qc)
                klen = 128 * len(kcs)
                masked = mask_emit(qc, kcs) if mask_emit else False
                for h in range(nheads):
                    hc, hr = h // 2, 64 * (h % 2)

                    def sfn(e, qc=qc, hc=hc, hr=hr, kcs=kcs):
                        pos = 0
                        inst = None
                        i = 0
                        while i < len(kcs):
                            j = i
                            while j + 1 < len(kcs) and kcs[j + 1] == kcs[j] + 1:
                                j += 1
                            rp = pos
                            rk = kcs[i] * 128
                            rlen = (j - i + 1) * 128
                            while rlen > 0:
                                n = min(512 - rp % 512, rlen)
                                inst = e.matmul(psA[:, rp:rp + n],
                                                arv(2, hc)[hr:hr + 64, qc * 128:(qc + 1) * 128],
                                                arv(4, hc)[hr:hr + 64, rk:rk + n],
                                                start=True, stop=True)
                                rp += n
                                rk += n
                                rlen -= n
                            pos += (j - i + 1) * 128
                            i = j + 1
                        return inst
                    m1 = qq("tensor", sfn, waits=w("ve", "ac"), inc="mm")
                    if masked:
                        v1 = qq("vector", lambda e, klen=klen: e.scalar_tensor_tensor(
                            S1[:, 0:klen], psA[:, 0:klen], 1.0, S2[:, 0:klen], ALU.bypass, ALU.add),
                            waits=[("mm", m1)] + (w("gd") if debug else []), inc="ve")
                        src = S1
                    else:
                        v1 = m1
                        src = psA
                    v2 = qq("vector", lambda e, src=src, klen=klen: e.reduce_max(
                        negm[:], src[:, 0:klen], AX.X, negate=True),
                        waits=([("ve", v1)] if masked else [("mm", m1)]), inc="ve")
                    a1 = qq("scalar", lambda e, src=src, klen=klen: e.activation(
                        PE1[:, 0:klen], src[:, 0:klen], AF.Exp, bias=negm[:], accum_out=sums[:]),
                        waits=[("ve", v2)], inc="ac")
                    v3 = qq("vector", lambda e: e.reciprocal(rcol[:], sums[:]),
                            waits=[("ac", a1)], inc="ve")
                    v4 = qq("vector", lambda e: e.tensor_scalar(DG[:], c_eye[:], rcol[:], None,
                                                                ALU.mult, ALU.bypass),
                            waits=[("ve", v3)], inc="ve")
                    ngrp = (len(kcs) + 3) // 4
                    evs = []
                    for gg in range(ngrp):
                        nk = min(4, len(kcs) - 4 * gg)
                        sslot = (gg % 2)

                        def ptfn(e, gg=gg, sslot=sslot, nk=nk):
                            for i in range(nk):
                                inst = e.matmul(Bk(sslot)[:, i * 128:(i + 1) * 128],
                                                PE1[:, (4 * gg + i) * 128:(4 * gg + i + 1) * 128],
                                                DG[:], start=(i == 0), stop=(i == nk - 1))
                            return inst
                        mw = [("ve", v4)] if gg < 2 else [("ve", evs[gg - 2])]
                        m2 = qq("tensor", ptfn, waits=mw, inc="mm")
                        ev = qq("vector", lambda e, gg=gg, sslot=sslot, nk=nk: e.tensor_copy(
                            PTB[:, 4 * gg:4 * gg + nk, :].rearrange("p a t -> p (a t)"),
                            Bk(sslot)[:, 0:nk * 128]), waits=[("mm", m2)], inc="ve")
                        evs.append(ev)

                    def avfn(e, h=h, hr=hr, kcs=kcs):
                        for i, kc in enumerate(kcs):
                            inst = e.matmul(psB[hr:hr + 64, 1024:1152],
                                            vtv(6, kc)[:, h * 64:h * 64 + 64],
                                            PTB[:, i, :], start=(i == 0), stop=(i == len(kcs) - 1))
                        return inst
                    m3 = qq("tensor", avfn, waits=[("ve", evs[-1])], inc="mm")
                    qq("scalar", lambda e, hc=hc, hr=hr, qc=qc: e.copy(
                        arv(out_lo, hc)[hr:hr + 64, qc * 128:(qc + 1) * 128],
                        psB[hr:hr + 64, 1024:1152]), waits=[("mm", m3)], inc="ac")

        def oproj(bi, src_lo, wname, bcol=None):
            for oc in range(C):
                wv = loadcast(d[wname][:, oc * 128:(oc + 1) * 128], C, 128)
                for nt in range(NT):
                    sl = slice(nt * 512, (nt + 1) * 512)
                    par = nt % 2

                    def ofn(e, wv=wv, sl=sl):
                        for cch in range(C):
                            inst = e.matmul(Bk(0), wv[:, cch, :], arv(src_lo, cch)[:, sl],
                                            start=(cch == 0), stop=(cch == C - 1))
                        return inst
                    m1 = qq("tensor", ofn, waits=w("ve", "ac"), inc="mm")
                    if bcol is not None:
                        qq("vector", lambda e, par=par, oc=oc: e.tensor_scalar(
                            OST[:, par, :], Bk(0), BIA[:, bcol + oc:bcol + oc + 1], None,
                            ALU.add, ALU.bypass),
                            waits=[("mm", m1)] + w("gd"), inc="ve")
                    else:
                        qq("vector", lambda e, par=par: e.tensor_copy(OST[:, par, :], Bk(0)),
                           waits=[("mm", m1)] + w("gd"), inc="ve")
                    qq("gpsimd", lambda e, bi=bi, oc=oc, sl=sl, par=par: e.dma_start(
                        out=d["xres"][bi, oc * 128:(oc + 1) * 128, sl],
                        in_=OST[:, par, :], accum_op=ALU.add),
                        waits=w("ve"), inc="gd", n=16)

        def ph_l0_attn(bi, xn_lo):
            qq("sync", lambda e: e.dma_start(out=BIA[:], in_=d["l0_bias"]),
               waits=w("ve", "ac", "mm"), inc="ldm", n=16)
            qkv_proj(xn_lo, "l0_wqt", "l0_wkt", "l0_wvt", 0.125, qb=0, kb=8, vb=16)
            if debug:
                for i in range(6):
                    qq("gpsimd", lambda e, i=i: e.dma_start(
                        out=d["dbgA"][i], in_=AR[2 + i][:].rearrange("p a t -> p (a t)")),
                        waits=w("ve", "ac", "mm"), inc="gd", n=16)
            attn_inner(xn_lo, lambda qc: list(range(TC)), None if False else (lambda qc, kcs: True))
            oproj(bi, xn_lo, "l0_wot", bcol=24)

        def ph_l3_attn(bi, xn_lo):
            qq("sync", lambda e: e.dma_start(out=BIA[:, 0:8], in_=d["l3_bias"]),
               waits=w("ve", "ac", "mm"), inc="ldm", n=16)
            qkv_proj(xn_lo, "l3_wq", "l3_wk", "l3_wv", 0.125)

            def l3_mask(qc, kcs):
                klen = 128 * len(kcs)
                qq("vector", lambda e, klen=klen: e.memset(S2[:, 0:klen], 0.0),
                   waits=w("ve", "ac"), inc="ve")
                qq("vector", lambda e, klen=klen: e.tensor_copy(
                    S2[:, klen - 128:klen], c_tril[:]), waits=w("ve"), inc="ve")
                return True
            attn_inner(xn_lo, lambda qc: list(range(qc + 1)), l3_mask)
            oproj(bi, xn_lo, "l3_wo", bcol=0)

        def ph_l3_quant():
            for ic in range(C):
                qq("sync", lambda e, ic=ic: e.dma_start(
                    out=WST[:, 0:1024], in_=d["l3_kern"][ic * 128:(ic + 1) * 128, :]),
                    waits=w("ve", "ac", "mm"), inc="ldm", n=16)
                qq("vector", lambda e, ic=ic: e.tensor_reduce(
                    btmp[:, ic:ic + 1], WST[:, 0:1024], AX.X, ALU.max,
                    apply_absolute_value=True), waits=w("ldm"), inc="ve")
            qq("vector", lambda e: e.reduce_max(tcol[:], btmp[:], AX.X), waits=w("ve"), inc="ve")
            qq("gpsimd", lambda e: e.dma_start(out=d["mrow"][0:128], in_=tcol[:]),
               waits=w("ve"), inc="gd", n=16)
            qq("gpsimd", lambda e: e.dma_start(
                out=U1[0:1, 0:128], in_=d["mrow"][0:128].rearrange("(o t) -> o t", o=1)),
                waits=w("gd"), inc="gd", n=16)
            qq("vector", lambda e: e.reduce_max(tAa[:], U1[0:1, 0:128], AX.X),
               waits=w("gd"), inc="ve")
            qq("vector", lambda e: e.tensor_scalar(tAb[:], tAa[:], 1.0 / 7.0, None, ALU.mult, ALU.bypass),
               waits=w("ve"), inc="ve")
            qq("vector", lambda e: e.reciprocal(midA[:], tAb[:]), waits=w("ve"), inc="ve")
            qq("tensor", lambda e: e.matmul(psB[0:128, 1024:1025], onesrow[:], tAb[:],
                                            start=True, stop=True), waits=w("ve"), inc="mm")
            qq("scalar", lambda e: e.copy(qcol[:], psB[0:128, 1024:1025]), waits=w("mm"), inc="ac")
            qq("tensor", lambda e: e.matmul(psB[0:128, 1024:1025], onesrow[:], midA[:],
                                            start=True, stop=True), waits=w("ac"), inc="mm")
            qq("scalar", lambda e: e.copy(tcol[:], psB[0:128, 1024:1025]), waits=w("mm"), inc="ac")
            for ic in range(C):
                qq("sync", lambda e, ic=ic: e.dma_start(
                    out=WST[:, 0:1024], in_=d["l3_kern"][ic * 128:(ic + 1) * 128, :]),
                    waits=w("ve", "ac", "mm"), inc="ldm", n=16)
                qq("vector", lambda e: e.tensor_scalar(S1[:, 0:1024], WST[:, 0:1024],
                                                       tcol[:], None, ALU.mult, ALU.bypass),
                   waits=w("ldm", "gd"), inc="ve")
                qq("vector", lambda e: e.tensor_scalar(S1[:, 1024:2048], S1[:, 0:1024],
                                                       12582912.0, -12582912.0, ALU.add, ALU.add),
                   waits=w("ve"), inc="ve")
                qq("vector", lambda e: e.tensor_scalar(S1[:, 0:1024], S1[:, 1024:2048],
                                                       7.0, -8.0, ALU.min, ALU.max),
                   waits=w("ve"), inc="ve")
                qq("vector", lambda e: e.tensor_scalar(PE1[:, 0:1024], S1[:, 0:1024],
                                                       qcol[:], None, ALU.mult, ALU.bypass),
                   waits=w("ve", "gd"), inc="ve")
                qq("gpsimd", lambda e, ic=ic: e.dma_start(
                    out=d["kqscr"][ic * 128:(ic + 1) * 128, :], in_=PE1[:, 0:1024]),
                    waits=w("ve"), inc="gd", n=16)

        def ph_l3_ffn(bi, xn_lo):
            qq("sync", lambda e: e.dma_start(out=BIA[:, 8:16], in_=d["l3_kbias"]),
               waits=w("ve", "ac", "mm"), inc="ldm", n=16)
            av = loadcast(d["l3_a"], C, 16)
            for nt in range(NT):
                sl = slice(nt * 512, (nt + 1) * 512)

                def ufn(e, sl=sl):
                    for cch in range(C):
                        inst = e.matmul(psB[0:16, 0:512], av[:, cch, :], arv(xn_lo, cch)[:, sl],
                                        start=(cch == 0), stop=(cch == C - 1))
                    return inst
                m1 = qq("tensor", ufn, waits=w("ve", "ac"), inc="mm")
                qq("scalar", lambda e, sl=sl: e.copy(LU[:, sl], psB[0:16, 0:512]),
                   waits=[("mm", m1)], inc="ac")
            qq("sync", lambda e: e.dma_start(out=U1[0:16, 0:1024] if False else WST[0:16, 0:1024],
                                             in_=d["l3_b"]),
               waits=w("ve", "ac", "mm"), inc="ldm", n=16)
            qq("vector", lambda e: e.tensor_copy(LB, WST[0:16, 0:1024]), waits=w("ldm"), inc="ve")
            for oc in range(C):
                bfv = WBF[:, 0:1024].rearrange("p (a w) -> p a w", w=128)
                qq("sync", lambda e, oc=oc, bfv=bfv: e.dma_start(
                    out=bfv, in_=d["kqscr"][:, oc * 128:(oc + 1) * 128].rearrange(
                        "(a p) w -> p a w", p=128)),
                    waits=w("ve", "mm"), inc="ldw", n=16)
                for nt in range(NT):
                    sl = slice(nt * 512, (nt + 1) * 512)
                    par = nt % 2

                    def ffn(e, bfv=bfv, sl=sl, oc=oc):
                        for cch in range(C):
                            e.matmul(Bk(0), bfv[:, cch, :], arv(xn_lo, cch)[:, sl],
                                     start=(cch == 0), stop=False)
                        return e.matmul(Bk(0), LB[:, oc * 128:(oc + 1) * 128], LU[:, sl],
                                        start=False, stop=True)
                    m1 = qq("tensor", ffn, waits=w("ldw", "ve", "ac"), inc="mm")
                    qq("vector", lambda e, par=par, oc=oc: e.tensor_scalar(
                        OST[:, par, :], Bk(0), BIA[:, 8 + oc:8 + oc + 1], None,
                        ALU.add, ALU.bypass),
                        waits=[("mm", m1)] + w("gd"), inc="ve")
                    qq("gpsimd", lambda e, bi=bi, oc=oc, sl=sl, par=par: e.dma_start(
                        out=d["xres"][bi, oc * 128:(oc + 1) * 128, sl],
                        in_=OST[:, par, :], accum_op=ALU.add),
                        waits=w("ve"), inc="gd", n=16)

        def ph_l1_attn(bi, xn_lo):
            qkv_proj(xn_lo, "l1_wqkv", "l1_wqkv", "l1_wqkv", 0.125,
                     qoff=0, koff=1024, voff=2048)

            def l1_kcs(qc):
                if qc == 0:
                    return [0]
                if qc == 1:
                    return [0, 1]
                return [0, qc - 1, qc]

            def l1_mask(qc, kcs):
                if qc == 0:
                    tiles = [c_tril]
                elif qc == 1:
                    tiles = [c_l1sp, c_tril]
                else:
                    tiles = [c_glob, c_triu, c_tril]
                for i, tl in enumerate(tiles):
                    qq("vector", lambda e, i=i, tl=tl: e.tensor_copy(
                        S2[:, i * 128:(i + 1) * 128], tl[:]), waits=w("ve", "ac"), inc="ve")
                return True
            attn_inner(xn_lo, l1_kcs, l1_mask)
            oproj(bi, xn_lo, "l1_wout")

        def ph_l1_ffn(bi, xn_lo):
            qq("sync", lambda e: e.dma_start(out=WST[0:1, 0:8], in_=d["l1_ebias"]),
               waits=w("ve", "ac", "mm"), inc="ldm", n=16)
            m2 = qq("tensor", lambda e: e.matmul(psB[0:128, 1024:1032], onesrow[:], WST[0:1, 0:8],
                                                 start=True, stop=True),
                    waits=w("ldm", "ac"), inc="mm")
            a2 = qq("scalar", lambda e: e.copy(btmp[:], psB[0:128, 1024:1032]),
                    waits=[("mm", m2)], inc="ac")
            for tc in range(TC):
                m3 = qq("tensor", lambda e, tc=tc: e.transpose(
                    psB[0:128, 1024 + (tc % 2) * 8:1032 + (tc % 2) * 8],
                    U1[0:8, tc * 128:(tc + 1) * 128], c_eye[0:8, 0:8]),
                    waits=w("ve", "ac"), inc="mm")
                qq("scalar", lambda e, tc=tc: e.copy(S1[:, tc * 8:tc * 8 + 8],
                                                     psB[0:128, 1024 + (tc % 2) * 8:1032 + (tc % 2) * 8]),
                   waits=[("mm", m3)], inc="ac")
                qq("vector", lambda e, tc=tc: e.tensor_tensor(
                    S1[:, 128 + tc * 8:136 + tc * 8], S1[:, tc * 8:tc * 8 + 8], btmp[:], ALU.add),
                    waits=w("ac"), inc="ve")
                qq("vector", lambda e, tc=tc: e.reduce_max(negm[:], S1[:, 128 + tc * 8:136 + tc * 8], AX.X),
                   waits=w("ve"), inc="ve")
                qq("vector", lambda e, tc=tc: e.tensor_scalar(
                    S1[:, 256 + tc * 8:264 + tc * 8], S1[:, 128 + tc * 8:136 + tc * 8],
                    negm[:], None, ALU.is_ge, ALU.bypass), waits=w("ve"), inc="ve")
                qq("vector", lambda e, tc=tc: e.scalar_tensor_tensor(
                    S1[:, 384 + tc * 8:392 + tc * 8], S1[:, 256 + tc * 8:264 + tc * 8], NEG,
                    S1[:, 128 + tc * 8:136 + tc * 8], ALU.mult, ALU.add), waits=w("ve"), inc="ve")
                qq("vector", lambda e, tc=tc: e.reduce_max(rcol[:], S1[:, 384 + tc * 8:392 + tc * 8], AX.X),
                   waits=w("ve"), inc="ve")
                qq("vector", lambda e, tc=tc: e.tensor_scalar(
                    S1[:, 256 + tc * 8:264 + tc * 8], S1[:, 128 + tc * 8:136 + tc * 8],
                    rcol[:], None, ALU.is_ge, ALU.bypass), waits=w("ve"), inc="ve")
                qq("vector", lambda e, tc=tc: e.tensor_tensor(
                    S1[:, 384 + tc * 8:392 + tc * 8], S1[:, 256 + tc * 8:264 + tc * 8],
                    S1[:, tc * 8:tc * 8 + 8], ALU.mult), waits=w("ve"), inc="ve")
            for tc in range(TC):
                m4 = qq("tensor", lambda e, tc=tc: e.transpose(
                    psB[0:8, (tc % 2) * 128:(tc % 2) * 128 + 128],
                    S1[:, 384 + tc * 8:392 + tc * 8], c_eye[:]),
                    waits=w("ve", "ac"), inc="mm")
                qq("scalar", lambda e, tc=tc: e.copy(
                    U1[0:8, tc * 128:(tc + 1) * 128],
                    psB[0:8, (tc % 2) * 128:(tc % 2) * 128 + 128]),
                    waits=[("mm", m4)], inc="ac")

            def hvv(oc):
                return AR[4 + oc // 4][:, oc % 4, :]

            for ex in range(8):
                for nt in range(NT):
                    sl = slice(nt * 512, (nt + 1) * 512)
                    qq("tensor", lambda e, ex=ex, sl=sl: e.matmul(
                        psB[:, 1024:1536], c_sel8[:, ex * 128:(ex + 1) * 128], U1[0:8, sl],
                        start=True, stop=True), waits=w("ve", "ac"), inc="mm")

                    def xefn(e, sl=sl):
                        for cch in range(C):
                            inst = e.tensor_tensor(arv(2, cch)[:, sl],
                                                   arv(xn_lo, cch)[:, sl],
                                                   psB[:, 1024:1536], ALU.mult)
                        return inst
                    qq("vector", xefn, waits=w("mm"), inc="ve")
                for oc in range(16):
                    wuv = loadcast(d["l1_wu"][ex, :, oc * 128:(oc + 1) * 128], C, 128, 0)
                    for nt in range(NT):
                        sl = slice(nt * 512, (nt + 1) * 512)

                        def z1fn(e, wuv=wuv, sl=sl):
                            for cch in range(C):
                                inst = e.matmul(A(0), wuv[:, cch, :], arv(2, cch)[:, sl],
                                                start=(cch == 0), stop=(cch == C - 1))
                            return inst
                        m1 = qq("tensor", z1fn, waits=w("ve", "ac"), inc="mm")
                        qq("scalar", lambda e: e.activation(tsil[:], A(0), AF.Relu),
                           waits=[("mm", m1)], inc="ac")
                        qq("vector", lambda e, oc=oc, sl=sl: e.tensor_tensor(
                            hvv(oc)[:, sl], tsil[:], tsil[:], ALU.mult),
                            waits=w("ac"), inc="ve")
                for oc in range(8):
                    wdv = loadcast(d["l1_wd"][ex, :, oc * 128:(oc + 1) * 128], 16, 128, 0)
                    for nt in range(NT):
                        sl = slice(nt * 512, (nt + 1) * 512)
                        par = nt % 2

                        def outfn(e, wdv=wdv, sl=sl):
                            for hc in range(16):
                                inst = e.matmul(Bk(0), wdv[:, hc, :], hvv(hc)[:, sl],
                                                start=(hc == 0), stop=(hc == 15))
                            return inst
                        m3 = qq("tensor", outfn, waits=w("ve", "ac"), inc="mm")
                        qq("scalar", lambda e, par=par: e.copy(OST[:, par, :], Bk(0)),
                           waits=[("mm", m3)] + w("gd"), inc="ac")
                        qq("gpsimd", lambda e, bi=bi, oc=oc, sl=sl, par=par: e.dma_start(
                            out=d["xres"][bi, oc * 128:(oc + 1) * 128, sl],
                            in_=OST[:, par, :], accum_op=ALU.add),
                            waits=w("ac"), inc="gd", n=16)

        def arv32(c):
            return AR[c // 4][:, c % 4, :]

        def ph_l2_ffn_s1(xn_lo):
            for nm, off in (("l2f_s1", 0), ("l2f_s2", 32), ("l2f_sh1", 64),
                            ("l2f_sh2", 96), ("l2f_s3", 128)):
                qq("sync", lambda e, nm=nm, off=off: e.dma_start(
                    out=SCL[:, off:off + (8 if nm == "l2f_s3" else 32)], in_=d[nm]),
                    waits=w("ve", "ac", "mm"), inc="ldm", n=16)
            for oc in range(32):
                w1v = loadcast(d["l2f_w1t"][:, oc * 128:(oc + 1) * 128], C, 128, 0)
                w2v = loadcast(d["l2f_w2t"][:, oc * 128:(oc + 1) * 128], C, 128, 1024)
                for nt in range(NT):
                    sl = slice(nt * 512, (nt + 1) * 512)

                    def z1fn(e, w1v=w1v, sl=sl):
                        for cch in range(C):
                            inst = e.matmul(A(0), w1v[:, cch, :], arv(xn_lo, cch)[:, sl],
                                            start=(cch == 0), stop=(cch == C - 1))
                        return inst
                    m1 = qq("tensor", z1fn, waits=w("ve", "ac", "ldm"), inc="mm")

                    def z2fn(e, w2v=w2v, sl=sl):
                        for cch in range(C):
                            inst = e.matmul(A(1), w2v[:, cch, :], arv(xn_lo, cch)[:, sl],
                                            start=(cch == 0), stop=(cch == C - 1))
                        return inst
                    m2 = qq("tensor", z2fn, inc="mm")
                    qq("scalar", lambda e, oc=oc: e.activation(tsil[:], A(0), AF.Silu,
                                                               scale=SCL[:, oc:oc + 1]),
                       waits=[("mm", m1)], inc="ac")
                    v1 = qq("vector", lambda e, oc=oc: e.scalar_tensor_tensor(
                        RQ[:].rearrange("p a t -> p (a t)")[:, 0:512], A(1),
                        SCL[:, 32 + oc:33 + oc], tsil[:], ALU.mult, ALU.mult),
                        waits=[("mm", m2)] + w("ac", "gd"), inc="ve")
                    qq("gpsimd", lambda e, oc=oc, sl=sl: e.dma_start(
                        out=d["hscr"][oc * 128:(oc + 1) * 128, sl],
                        in_=RQ[:].rearrange("p a t -> p (a t)")[:, 0:512]),
                        waits=[("ve", v1)], inc="gd", n=16)

        def ph_l2_qin(xn_lo):
            for oc in range(C):
                wv = loadcast(d["l2_wqt"][:, oc * 128:(oc + 1) * 128], C, 128)
                for nt in range(NT):
                    sl = slice(nt * 512, (nt + 1) * 512)
                    slot = (nt % 2) * 512

                    def pfn(e, wv=wv, sl=sl, slot=slot):
                        for cch in range(C):
                            inst = e.matmul(psA[:, slot:slot + 512], wv[:, cch, :],
                                            arv(xn_lo, cch)[:, sl],
                                            start=(cch == 0), stop=(cch == C - 1))
                        return inst
                    m1 = qq("tensor", pfn, waits=w("ve", "ac"), inc="mm")
                    a1 = qq("scalar", lambda e, oc=oc, sl=sl, slot=slot: e.copy(
                        arv(2, oc)[:, sl], psA[:, slot:slot + 512]),
                        waits=[("mm", m1)], inc="ac")
                qq("gpsimd", lambda e, oc=oc: e.dma_start(
                    out=d["qscr"][oc * 128:(oc + 1) * 128, :], in_=arv(2, oc)[:]),
                    waits=w("ac"), inc="gd", n=16)

        def ph_l2_dx():
            h2 = AR[4]
            for oc2 in range(2):
                wv = loadcast(d["l2_wopt"][:, oc2 * 128:(oc2 + 1) * 128], C, 128)
                for nt in range(NT):
                    sl = slice(nt * 512, (nt + 1) * 512)
                    for cch in range(C):
                        qq("vector", lambda e, cch=cch, sl=sl: e.tensor_scalar(
                            RQ[:, cch % 2, :], arv(2, cch)[:, sl], 0.0, None,
                            ALU.max, ALU.bypass), waits=w("mm", "gd"), inc="ve")
                        qq("tensor", lambda e, wv=wv, cch=cch: e.matmul(
                            psB[:, 0:512], wv[:, cch, :], RQ[:, cch % 2, :],
                            start=(cch == 0), stop=(cch == C - 1)),
                            waits=w("ve", "ac"), inc="mm")
                    qq("scalar", lambda e, oc2=oc2, sl=sl: e.copy(h2[:, oc2, sl], psB[:, 0:512]),
                       waits=w("mm"), inc="ac")
            qq("sync", lambda e: e.dma_start(out=WST[0:1, 0:1], in_=d["l2_b2"]),
               waits=w("ve", "ac", "mm"), inc="ldm", n=16)
            m1 = qq("tensor", lambda e: e.matmul(psB[0:128, 1024:1025], onesrow[:], WST[0:1, 0:1],
                                                 start=True, stop=True), waits=w("ldm"), inc="mm")
            qq("scalar", lambda e: e.copy(btmp[:, 3:4], psB[0:128, 1024:1025]),
               waits=[("mm", m1)], inc="ac")
            qq("sync", lambda e: e.dma_start(out=WST[:, 0:2], in_=d["l2_w2"]),
               waits=w("ve", "ac", "mm"), inc="ldm", n=16)
            qq("vector", lambda e: e.tensor_copy(wi2b[:, 0:2, :],
                                                 WST[:, 0:2].rearrange("p (a w) -> p a w", w=1)),
               waits=w("ldm"), inc="ve")
            for tc in range(TC):
                def dfn(e, tc=tc):
                    for a in range(2):
                        inst = e.matmul(psB[:, 1024 + (tc % 2):1025 + (tc % 2)],
                                        h2[:, a, tc * 128:(tc + 1) * 128], wi2b[:, a, :],
                                        start=(a == 0), stop=(a == 1))
                    return inst
                m2 = qq("tensor", dfn, waits=w("ve", "ac"), inc="mm")
                a2 = qq("scalar", lambda e, tc=tc: e.activation(
                    rcol[:], psB[:, 1024 + (tc % 2):1025 + (tc % 2)], AF.Sigmoid,
                    bias=btmp[:, 3:4]), waits=[("mm", m2)], inc="ac")
                for gi, wind in enumerate((32.0, 64.0, 128.0, 256.0)):
                    qq("vector", lambda e, tc=tc, gi=gi, wind=wind: e.tensor_scalar(
                        DXG[:, tc, gi:gi + 1], rcol[:], 2048.0, wind, ALU.mult, ALU.add),
                        waits=[("ac", a2)] + w("ve"), inc="ve")

        def ph_l2_stage(srcname, dstname, wname, soff):
            ld0 = qq("sync", lambda e: e.dma_start(
                out=AR[0][:], in_=d[srcname][0:512, :].rearrange("(a p) t -> p a t", p=128)),
                waits=w("ve", "ac", "mm", "gd"), inc="lds", n=16)
            for i in range(1, 8):
                qq("sync", lambda e, i=i: e.dma_start(
                    out=AR[i][:], in_=d[srcname][i * 512:(i + 1) * 512, :].rearrange(
                        "(a p) t -> p a t", p=128)), inc="lds", n=16)
            ldall = Qz.c("lds")
            for oc in range(32):
                hb = []
                for half in range(2):
                    hv = loadcast(d[wname][half * 2048:(half + 1) * 2048,
                                           oc * 128:(oc + 1) * 128], 16, 128, 0)
                    hb.append(hv)

                    def zfn(e, hv=hv, half=half):
                        for ic in range(16):
                            for nt in range(NT):
                                inst = e.matmul(A(nt), hv[:, ic, :],
                                                arv32(half * 16 + ic)[:, nt * 512:(nt + 1) * 512],
                                                start=(half == 0 and ic == 0),
                                                stop=(half == 1 and ic == 15))
                        return inst
                    qq("tensor", zfn, waits=[("lds", ldall)] + w("ve", "ac"), inc="mm")
                for nt in range(NT):
                    a1 = qq("scalar", lambda e, oc=oc, nt=nt: e.activation(
                        tsil[:], A(nt), AF.Gelu, scale=SCL[:, soff + oc:soff + oc + 1]),
                        waits=w("mm", "gd"), inc="ac")
                    v1 = qq("vector", lambda e: e.tensor_copy(
                        RQ[:].rearrange("p a t -> p (a t)")[:, 0:512], tsil[:]),
                        waits=[("ac", a1)] + w("gd"), inc="ve")
                    qq("gpsimd", lambda e, oc=oc, nt=nt: e.dma_start(
                        out=d[dstname][oc * 128:(oc + 1) * 128, nt * 512:(nt + 1) * 512],
                        in_=RQ[:].rearrange("p a t -> p (a t)")[:, 0:512]),
                        waits=[("ve", v1)], inc="gd", n=16)

        def ph_l2_stage4(bi, srcname):
            ld0 = qq("sync", lambda e: e.dma_start(
                out=AR[0][:], in_=d[srcname][0:512, :].rearrange("(a p) t -> p a t", p=128)),
                waits=w("ve", "ac", "mm", "gd"), inc="lds", n=16)
            for i in range(1, 8):
                qq("sync", lambda e, i=i: e.dma_start(
                    out=AR[i][:], in_=d[srcname][i * 512:(i + 1) * 512, :].rearrange(
                        "(a p) t -> p a t", p=128)), inc="lds", n=16)
            ldall = Qz.c("lds")
            for oc in range(C):
                hb = []
                for half in range(2):
                    hv = loadcast(d["l2f_w3t"][half * 2048:(half + 1) * 2048,
                                               oc * 128:(oc + 1) * 128], 16, 128, 0)
                    hb.append(hv)

                    def zfn(e, hv=hv, half=half):
                        for ic in range(16):
                            for nt in range(NT):
                                inst = e.matmul(A(nt), hv[:, ic, :],
                                                arv32(half * 16 + ic)[:, nt * 512:(nt + 1) * 512],
                                                start=(half == 0 and ic == 0),
                                                stop=(half == 1 and ic == 15))
                        return inst
                    qq("tensor", zfn, waits=[("lds", ldall)] + w("ve", "ac"), inc="mm")
                for nt in range(NT):
                    par = nt % 2
                    v1 = qq("vector", lambda e, oc=oc, nt=nt, par=par: e.tensor_scalar(
                        OST[:, par, :], A(nt), SCL[:, 128 + oc:129 + oc], None,
                        ALU.mult, ALU.bypass), waits=w("mm", "gd"), inc="ve")
                    qq("gpsimd", lambda e, bi=bi, oc=oc, nt=nt, par=par: e.dma_start(
                        out=d["xres"][bi, oc * 128:(oc + 1) * 128, nt * 512:(nt + 1) * 512],
                        in_=OST[:, par, :], accum_op=ALU.add),
                        waits=[("ve", v1)], inc="gd", n=16)

        def ph_l2_attn(bi):
            for oc in range(C):
                qq("sync", lambda e, oc=oc: e.dma_start(
                    out=arv(2, oc)[:], in_=d["qscr"][oc * 128:(oc + 1) * 128, :]),
                    waits=w("ve", "ac", "mm", "gd"), inc="lds", n=16)
            qq("sync", lambda e: e.dma_start(out=BIA[:, 0:24], in_=d["l2_bqkv"]),
               waits=w("lds"), inc="ldm", n=16)
            qsc = 2.0 / float(np.sqrt(2048.0))
            qkv_proj(2, "l2_wqkvt", "l2_wqkvt", "l2_wqkvt", qsc,
                     qb=0, kb=8, vb=16, qoff=0, koff=1024, voff=2048,
                     q_lo=0, k_lo=4, vt_lo=6)
            for g4 in range(4):
                for qc in range(TC):
                    kcs = list(range(qc, TC))
                    klen = 128 * len(kcs)
                    qq("vector", lambda e, qc=qc: e.tensor_scalar(
                        qcol[:], c_pid[:], float(128 * qc), None, ALU.add, ALU.bypass),
                        waits=w("ve", "ac"), inc="ve")
                    qq("vector", lambda e, qc=qc, klen=klen: e.tensor_scalar(
                        WST[:, 0:klen], c_iotab[:, qc * 128:2048], qcol[:], None,
                        ALU.subtract, ALU.bypass), waits=w("ve"), inc="ve")
                    qq("vector", lambda e, qc=qc, g4=g4, klen=klen: e.tensor_scalar(
                        S2[:, 0:klen], WST[:, 0:klen], DXG[:, qc, g4:g4 + 1], NEG,
                        ALU.is_gt, ALU.mult), waits=w("ve"), inc="ve")
                    qq("vector", lambda e: e.tensor_tensor(
                        tsil[:, 0:128], S2[:, 0:128], c_triu[:], ALU.add),
                        waits=w("ve"), inc="ve")
                    qq("vector", lambda e: e.tensor_copy(S2[:, 0:128], tsil[:, 0:128]),
                       waits=w("ve"), inc="ve")

                    def sfn(e, g4=g4, qc=qc, klen=klen):
                        pos = 0
                        while pos < klen:
                            n = min(512, klen - pos)
                            for ci, cch in enumerate((2 * g4, 2 * g4 + 1)):
                                inst = e.matmul(psA[:, pos:pos + n],
                                                arv(0, cch)[:, qc * 128:(qc + 1) * 128],
                                                arv(4, cch)[:, qc * 128 + pos:qc * 128 + pos + n],
                                                start=(ci == 0), stop=(ci == 1))
                            pos += n
                        return inst
                    m1 = qq("tensor", sfn, waits=w("ve", "ac"), inc="mm")
                    v1 = qq("vector", lambda e, klen=klen: e.scalar_tensor_tensor(
                        WST[:, 0:klen], psA[:, 0:klen], 1.0, S2[:, 0:klen], ALU.bypass, ALU.add),
                        waits=[("mm", m1)], inc="ve")
                    v2 = qq("vector", lambda e, klen=klen: e.reduce_max(
                        negm[:], WST[:, 0:klen], AX.X, negate=True),
                        waits=[("ve", v1)], inc="ve")
                    a1 = qq("scalar", lambda e, klen=klen: e.activation(
                        PE1[:, 0:klen], WST[:, 0:klen], AF.Exp, bias=negm[:], accum_out=sums[:]),
                        waits=[("ve", v2)], inc="ac")
                    v3 = qq("vector", lambda e: e.reciprocal(rcol[:], sums[:]),
                            waits=[("ac", a1)], inc="ve")
                    v4 = qq("vector", lambda e: e.tensor_scalar(DG[:], c_eye[:], rcol[:], None,
                                                                ALU.mult, ALU.bypass),
                            waits=[("ve", v3)], inc="ve")
                    ngrp = (len(kcs) + 3) // 4
                    evs = []
                    for gg in range(ngrp):
                        nk = min(4, len(kcs) - 4 * gg)
                        sslot = (gg % 2)

                        def ptfn(e, gg=gg, sslot=sslot, nk=nk):
                            for i in range(nk):
                                inst = e.matmul(Bk(sslot)[:, i * 128:(i + 1) * 128],
                                                PE1[:, (4 * gg + i) * 128:(4 * gg + i + 1) * 128],
                                                DG[:], start=(i == 0), stop=(i == nk - 1))
                            return inst
                        mw = [("ve", v4)] if gg < 2 else [("ve", evs[gg - 2])]
                        m2 = qq("tensor", ptfn, waits=mw, inc="mm")
                        ev = qq("vector", lambda e, gg=gg, sslot=sslot, nk=nk: e.tensor_copy(
                            PTB[:, 4 * gg:4 * gg + nk, :].rearrange("p a t -> p (a t)"),
                            Bk(sslot)[:, 0:nk * 128]), waits=[("mm", m2)], inc="ve")
                        evs.append(ev)
                    for half in range(2):
                        def avfn(e, g4=g4, half=half, kcs=kcs):
                            for i, kc in enumerate(kcs):
                                inst = e.matmul(psB[:, 1024 + half * 128:1152 + half * 128],
                                                vtv(6, kc)[:, g4 * 256 + half * 128:g4 * 256 + (half + 1) * 128],
                                                PTB[:, i, :], start=(i == 0), stop=(i == len(kcs) - 1))
                            return inst
                        m3 = qq("tensor", avfn, waits=[("ve", evs[-1])] + w("ac"), inc="mm")
                        dst = (S1 if half == 0 else SCA)
                        qq("scalar", lambda e, dst=dst, qc=qc, half=half: e.copy(
                            dst[:, qc * 128:(qc + 1) * 128],
                            psB[:, 1024 + half * 128:1152 + half * 128]),
                            waits=[("mm", m3)] + w("gd"), inc="ac")
                for half in range(2):
                    src = (S1 if half == 0 else SCA)
                    qq("gpsimd", lambda e, bi=bi, g4=g4, half=half, src=src: e.dma_start(
                        out=d["xres"][bi, (2 * g4 + half) * 128:(2 * g4 + half + 1) * 128, :],
                        in_=src[:], accum_op=ALU.add),
                        waits=w("ac", "ve"), inc="gd", n=16)

        ph_consts()
        if 0 in layers:
            ph_gating()
            ph_t8()
            for bi in range(NB):
                xsrc = lambda cch, bi=bi: d["xin"][bi, cch * 128:(cch + 1) * 128, :]
                ph_ln(xsrc, 0)
                ph_l0_keymask(0)
                ph_l0_ffn(bi, 0)
                ph_l0_attn(bi, 0)
        if 1 in layers:
            for bi in range(NB):
                xsrc = lambda cch, bi=bi: (d["xres"] if 0 in layers
                                           else d["xin"])[bi, cch * 128:(cch + 1) * 128, :]
                ph_ln(xsrc, 0, gate=True)
                ph_l1_ffn(bi, 0)
                ph_l1_attn(bi, 0)
        if 2 in layers:
            for bi in range(NB):
                xsrc = lambda cch, bi=bi: (d["xres"] if (0 in layers or 1 in layers)
                                           else d["xin"])[bi, cch * 128:(cch + 1) * 128, :]
                ph_ln(xsrc, 0)
                ph_l2_ffn_s1(0)
                ph_l2_qin(0)
                ph_l2_dx()
                ph_l2_stage("hscr", "gscr", "l2f_h1t", 64)
                ph_l2_stage("gscr", "hscr", "l2f_h2t", 96)
                ph_l2_stage4(bi, "hscr")
                ph_l2_attn(bi)
        if 3 in layers:
            ph_l3_quant()
            for bi in range(NB):
                xsrc = lambda cch, bi=bi: (d["xres"] if 0 in layers or 1 in layers or 2 in layers
                                           else d["xin"])[bi, cch * 128:(cch + 1) * 128, :]
                ph_ln(xsrc, 0)
                ph_l3_ffn(bi, 0)
                ph_l3_attn(bi, 0)
        qq("sync", lambda e: e.dma_start(out=d["y"], in_=d["xres"]),
           waits=w("gd", "ve", "ac", "mm", "ldc", "lds", "ldw", "ldm"), inc="st", n=16)
        qq("gpsimd", lambda e: e.nop(), waits=w("st"))
        Qz.play(block)
    return nc



def host_prep(inputs, layers=(0, 1, 2, 3)):
    f = np.ascontiguousarray
    x = np.asarray(inputs["L0_attn_x"], dtype=np.float32)
    xg = f(x.transpose(0, 2, 1))

    def tr(a):
        return f(np.asarray(a, dtype=np.float32).T)

    def asf(a):
        return f(np.asarray(a, dtype=np.float32))

    def colpack(b):
        return f(np.asarray(b, np.float32).reshape(-1, 128).T)

    com = {}
    jj = np.arange(128, dtype=np.float32)
    com["eye_f"] = f(np.eye(128, dtype=np.float32))
    com["iotab"] = f(np.broadcast_to(np.arange(T, dtype=np.float32)[None, :], (128, T)).copy())
    com["pid"] = f(jj[:, None].copy())
    dm = jj[None, :] - jj[:, None]
    import ml_dtypes
    com["trilm"] = f(np.where(dm > 0, NEG, 0.0).astype(ml_dtypes.bfloat16))
    com["trium"] = f(np.where(dm < 0, NEG, 0.0).astype(ml_dtypes.bfloat16))
    com["globm"] = f(np.where(np.broadcast_to(jj[None, :], (128, 128)) < 64, 0.0, NEG).astype(ml_dtypes.bfloat16))
    com["l1spm"] = f(np.where((dm >= 0) | (np.broadcast_to(jj[None, :], (128, 128)) < 64), 0.0, NEG).astype(ml_dtypes.bfloat16))
    s8 = np.zeros((8, 1024), dtype=np.float32)
    for e in range(8):
        s8[e, e * 128:(e + 1) * 128] = 1.0
    com["sel8"] = s8
    e16 = np.zeros((8, 128), dtype=np.float32)
    for e in range(8):
        e16[e, e::8] = 1.0
    com["e16"] = e16
    com["e16t"] = f(e16.T)

    if 0 in layers:
        com["xg"] = xg
        com["l0_wqt"] = tr(inputs["L0_attn_wq"])
        com["l0_wkt"] = tr(inputs["L0_attn_wk"])
        com["l0_wvt"] = tr(inputs["L0_attn_wv"])
        com["l0_wot"] = tr(inputs["L0_attn_wo"])
        com["l0_wi1t"] = tr(inputs["L0_attn_wi1"])
        com["l0_wi2"] = f(np.asarray(inputs["L0_attn_wi2"], np.float32).reshape(4, 128).T)
        com["l0_wg"] = asf(inputs["L0_ffn_wg"])
        com["l0_w1"] = asf(inputs["L0_ffn_w1"])
        com["l0_w3"] = asf(inputs["L0_ffn_w3"])
        com["l0_w2"] = asf(inputs["L0_ffn_w2"])
        com["l0_bi1"] = colpack(inputs["L0_attn_bi1"])
        com["l0_bias"] = f(np.concatenate(
            [colpack(inputs["L0_attn_bq"]), colpack(inputs["L0_attn_bk"]),
             colpack(inputs["L0_attn_bv"]), colpack(inputs["L0_attn_bo"])], axis=1))

    if 1 in layers:
        com["l1_wqkv"] = asf(inputs["L1_attn_w_qkv"])
        com["l1_wout"] = asf(inputs["L1_attn_w_out"])
        com["l1_gwt"] = tr(inputs["L1_ffn_gate_w"])
        com["l1_wu"] = f(np.asarray(inputs["L1_ffn_w_up"], np.float32).transpose(0, 2, 1))
        com["l1_wd"] = f(np.asarray(inputs["L1_ffn_w_down"], np.float32).transpose(0, 2, 1))
        com["l1_ebias"] = f(np.asarray(inputs["L1_ffn_expert_bias"], np.float32).reshape(1, 8))
    if 2 in layers:
        com["l2_wqt"] = tr(inputs["L2_attn_Wq"])
        com["l2_wqkvt"] = tr(inputs["L2_attn_Wqkv"])
        com["l2_bqkv"] = colpack(inputs["L2_attn_bqkv"])
        com["l2_wopt"] = tr(inputs["L2_attn_Wop"])
        com["l2_w2"] = f(np.asarray(inputs["L2_attn_W2"], np.float32).reshape(2, 128).T)
        com["l2_b2"] = f(np.asarray(inputs["L2_attn_b2"], np.float32).reshape(1, 1))
        com["l2f_w1t"] = tr(inputs["L2_ffn_w1_w"])
        com["l2f_w2t"] = tr(inputs["L2_ffn_w2_w"])
        com["l2f_h1t"] = tr(inputs["L2_ffn_h1_w"])
        com["l2f_h2t"] = tr(inputs["L2_ffn_h2_w"])
        com["l2f_w3t"] = tr(inputs["L2_ffn_w3_w"])
        com["l2f_s1"] = colpack(inputs["L2_ffn_w1_s"])
        com["l2f_s2"] = colpack(inputs["L2_ffn_w2_s"])
        com["l2f_sh1"] = colpack(inputs["L2_ffn_h1_s"])
        com["l2f_sh2"] = colpack(inputs["L2_ffn_h2_s"])
        com["l2f_s3"] = colpack(inputs["L2_ffn_w3_s"])
    if 3 in layers:
        com["l3_wq"] = asf(inputs["L3_attn_Wq"])
        com["l3_wk"] = asf(inputs["L3_attn_Wk"])
        com["l3_wv"] = asf(inputs["L3_attn_Wv"])
        com["l3_wo"] = asf(inputs["L3_attn_Wo"])
        com["l3_bias"] = colpack(inputs["L3_attn_bo"])
        com["l3_kern"] = asf(inputs["L3_ffn_kernel"])
        com["l3_kbias"] = colpack(inputs["L3_ffn_bias"])
        com["l3_a"] = asf(inputs["L3_ffn_a"])
        com["l3_b"] = asf(inputs["L3_ffn_b"])

    in_maps = []
    for cc in range(NCORES):
        m = dict(com)
        m["xin"] = f(xg[2 * cc:2 * cc + 2])
        sb0 = np.zeros((128, 8), dtype=np.float32)
        sb1 = np.zeros((128, 8), dtype=np.float32)
        for e in range(8):
            sb0[(2 * cc) * 8 + e, e] = 1.0
            sb1[(2 * cc + 1) * 8 + e, e] = 1.0
        m["selb0"] = sb0
        m["selb1"] = sb1
        in_maps.append(m)
    return in_maps


_PROG_CACHE = {}


def kernel(**inputs):
    layers = (0, 1, 2, 3)
    key = ("prog", layers)
    if key not in _PROG_CACHE:
        _PROG_CACHE[key] = build_program(layers)
    nc = _PROG_CACHE[key]
    in_maps = host_prep(inputs, layers)
    res = run_bass_kernel_spmd(nc, in_maps, core_ids=list(range(NCORES)))
    out = np.empty((B, T, D), dtype=np.float32)
    for cc in range(NCORES):
        y = res.results[cc]["y"]
        out[2 * cc:2 * cc + 2] = y.transpose(0, 2, 1)
    return out
